# version 06
# baseline (speedup 1.0000x reference)
import numpy as np
from contextlib import ExitStack

import concourse.bass as bass
import concourse.mybir as mybir
from concourse.bass_utils import run_bass_kernel_spmd

F32 = mybir.dt.float32
BF16 = mybir.dt.bfloat16
AF = mybir.ActivationFunctionType
ALU = mybir.AluOpType

B = 2
NB = 2048
NT = 16
D = 1024
C8 = 8
H = 16
M = 256
NORM = float(64.0 ** -0.25)
LN116 = float(np.log(1.0 / 16.0))
EPSC = 0.0625 * 0.001
LN_EPS = 1.0e-5
NEG_INF = -3.0e38


class Ctx:

    def __init__(self, role, eng, S):
        self.role = role
        self.e = eng
        self.S = S
        self.L = {k: 0 for k in S}

    def w2(self, role, sem, val):
        if self.role == role and val > 0:
            self.e.wait_ge(self.S[sem], val)

    def dma(self, q, out, in_, transpose=False, sem=None):
        if sem is None:
            sem = {"y": "ld", "g": "gld", "s": "sld"}[q]
        self.L[sem] += 16
        if self.role == q:
            kw = {"transpose": True} if transpose else {}
            self.e.dma_start(out=out, in_=in_, **kw).then_inc(self.S[sem], 16)
        return self.L[sem]

    def mm(self, out, l, r, start, stop, tp=None, inc=False):
        if inc:
            self.L["mm"] += 1
        if self.role == "t":
            kw = {}
            if tp is not None:
                kw["tile_position"] = tp
            i = self.e.matmul(out, l, r, start=start, stop=stop, **kw)
            if inc:
                i.then_inc(self.S["mm"], 1)
        return self.L["mm"]

    def tr(self, out, in_, ident, inc=False):
        if inc:
            self.L["mm"] += 1
        if self.role == "t":
            i = self.e.transpose(out, in_, ident)
            if inc:
                i.then_inc(self.S["mm"], 1)
        return self.L["mm"]

    def act(self, out, in_, func, bias=0.0, scale=1.0, accum_out=None, inc=False):
        if inc:
            self.L["act"] += 1
        if self.role == "s":
            kw = {}
            if accum_out is not None:
                kw["accum_out"] = accum_out
            i = self.e.activation(out=out, in_=in_, func=func, bias=bias,
                                  scale=scale, **kw)
            if inc:
                i.then_inc(self.S["act"], 1)
        return self.L["act"]

    def v(self, fname, *a, inc=False, **kw):
        if inc:
            self.L["dve"] += 1
        if self.role == "v":
            i = getattr(self.e, fname)(*a, **kw)
            if inc:
                i.then_inc(self.S["dve"], 1)
        return self.L["dve"]

    def g(self, fname, *a, inc=False, **kw):
        if inc:
            self.L["gp"] += 1
        if self.role == "g":
            i = getattr(self.e, fname)(*a, **kw)
            if inc:
                i.then_inc(self.S["gp"], 1)
        return self.L["gp"]

    def coll(self, ins, outs):
        self.L["cc"] += 1
        if self.role == "g":
            self.e.collective_compute(
                "AllGather", ALU.bypass, replica_groups=[list(range(8))],
                ins=ins, outs=outs,
            ).then_inc(self.S["cc"], 1)
        return self.L["cc"]


def build_nc():
    nc = bass.Bass("TRN2", target_bir_lowering=False)

    d_xfm = nc.dram_tensor("xfm", [D, B * NB], F32, kind="ExternalInput").ap()
    d_xtm = nc.dram_tensor("xtm", [B * NB, D], F32, kind="ExternalInput").ap()
    d_wqt = nc.dram_tensor("wqt", [D, D], F32, kind="ExternalInput").ap()
    d_wkt = nc.dram_tensor("wkt", [D, D], F32, kind="ExternalInput").ap()
    d_wvt = nc.dram_tensor("wvt", [D, D], F32, kind="ExternalInput").ap()
    d_wot = nc.dram_tensor("wot", [D, D], F32, kind="ExternalInput").ap()
    d_projtd = nc.dram_tensor("projtd", [128, M], F32, kind="ExternalInput").ap()
    d_half1 = nc.dram_tensor("half1", [128, 128], F32, kind="ExternalInput").ap()
    d_ident = nc.dram_tensor("ident", [128, 128], F32, kind="ExternalInput").ap()
    d_bq = nc.dram_tensor("bq8", [128, 8], F32, kind="ExternalInput").ap()
    d_bk = nc.dram_tensor("bk8", [128, 8], F32, kind="ExternalInput").ap()
    d_borep = nc.dram_tensor("borep", [128, D], F32, kind="ExternalInput").ap()
    d_bfrrep = nc.dram_tensor("bfrrep", [128, D], F32, kind="ExternalInput").ap()
    d_bvfix = nc.dram_tensor("bvfix", [1, H * 65], F32, kind="ExternalInput").ap()
    d_g1r = nc.dram_tensor("g1r", [128, 32], F32, kind="ExternalInput").ap()
    d_g1i = nc.dram_tensor("g1i", [128, 32], F32, kind="ExternalInput").ap()
    d_tfr = nc.dram_tensor("tfr", [128, 16 * 512], F32, kind="ExternalInput").ap()
    d_tfi = nc.dram_tensor("tfi", [128, 16 * 512], F32, kind="ExternalInput").ap()

    d_attnh = nc.dram_tensor("attnh", [C8, B * NB, 128], BF16, kind="Internal").ap()
    d_x1h = nc.dram_tensor("x1h", [B * NB, D], F32, kind="Internal").ap()
    d_x1bh = nc.dram_tensor("x1bh", [C8, B * NB, 128], BF16, kind="Internal").ap()
    d_ccin = nc.dram_tensor("ccin", [1, 1], F32, kind="Internal").ap()
    d_ccout = nc.dram_tensor("ccout", [1, 8], F32, kind="Internal").ap()

    d_y = nc.dram_tensor("y", [B * NB, D], F32, kind="ExternalOutput").ap()

    es = ExitStack()
    sb = lambda n, s, d: es.enter_context(nc.sbuf_tensor(n, s, d))
    ps = lambda n, s, d: es.enter_context(nc.psum_tensor(n, s, d))

    ar1 = sb("ar1", [128, 16640], BF16)
    ar2 = sb("ar2", [128, 16640], BF16)
    ar3 = sb("ar3", [128, 16640], BF16)
    ar4 = sb("ar4", [128, 16640], BF16)

    k0 = ar1[:, 0:16384].rearrange("p (c n) -> p c n", c=C8)
    q1 = ar1[:, 0:16384].rearrange("p (c n) -> p c n", c=C8)
    art = ar1[:, 0:16384].rearrange("p (c n) -> p c n", c=C8)
    k1 = ar2[:, 0:16384].rearrange("p (c n) -> p c n", c=C8)
    tfr_s = ar2[:, 0:8192]
    tfi_s = ar2[:, 8192:16384]
    v0 = ar3[:, 0:16640].rearrange("p (t e) -> p t e", t=NT)
    raw0 = ar3[:, 0:16384].rearrange("p (t f) -> p t f", t=NT)
    ar3f = ar3[:, 0:16384].bitcast(F32)
    ar2f = ar2[:, 0:16384].bitcast(F32)
    xtmb = [ar2f[:, i * 1024:(i + 1) * 1024] for i in range(4)]
    tbuf = [ar2f[:, (4 + i) * 1024:(5 + i) * 1024] for i in range(4)]
    x1pre = [ar3f[:, i * 1024:(i + 1) * 1024] for i in range(4)]
    x1t = [ar3f[:, (4 + i) * 1024:(5 + i) * 1024] for i in range(4)]
    q0 = ar4[:, 0:16384].rearrange("p (c n) -> p c n", c=C8)
    v1 = ar4[:, 0:16640].rearrange("p (t e) -> p t e", t=NT)
    raw1 = ar4[:, 0:16384].rearrange("p (t f) -> p t f", t=NT)
    x1bfb = ar4[:, 0:4096].rearrange("p (t f) -> p t f", t=4)
    sqscr = ar4[:, 4096:6144].bitcast(F32)
    kv = [k0, k1]
    qv = [q0, q1]
    vv = [v0, v1]
    rawv = [raw0, raw1]

    wg = sb("wg", [128, C8, 1024], BF16)
    xsb = [sb("xs0", [128, C8, 512], BF16), sb("xs1", [128, C8, 512], BF16)]
    wtv = [xsb[0], xsb[1]]
    qfd = sb("qf", [128, 2, 2, NB], BF16)
    qfb = [qfd[:, 0], qfd[:, 1]]
    _qfflat = qfd.rearrange("p a b n -> p (a b n)")
    ksqsb = [_qfflat[:, 0:1024].rearrange("p (c n) -> p c n", c=C8),
             _qfflat[:, 1024:2048].rearrange("p (c n) -> p c n", c=C8)]
    kfb = [sb("kf0", [128, M], BF16), sb("kf1", [128, M], BF16)]
    ctxbf = sb("ctxbf", [128, H, 130], BF16)
    ctxT = sb("ctxT", [65, M], F32)
    diag = sb("diag", [128, 512], F32)
    dadj = sb("dadj", [128, 512], F32)
    vsum = sb("vsum", [1, H * 65], F32)
    vsc = sb("vsc", [1, H * 65], BF16)
    kcsrow = sb("kcsrow", [1, M], BF16)
    srow = sb("srow", [1, H * 65], BF16)
    dinv = sb("dinv", [128, NT], F32)
    lnb = sb("lnb", [128, 64], F32)
    maxcol = sb("maxcol", [128, 1], F32)
    tmax = sb("tmax", [128, 1], F32)
    mx = sb("mx", [1, 128], F32)
    mloc = sb("mloc", [1, 1], F32)
    gat = sb("gat", [1, 8], F32)
    st11 = sb("st11", [1, 1], F32)
    stabv = sb("stabv", [128, 1], F32)

    projtd = sb("projtd_s", [128, M], BF16)
    half1 = sb("half1_s", [128, 128], BF16)
    identf = sb("identf", [128, 128], F32)
    bq8 = sb("bq8_s", [128, 8], F32)
    bk8 = sb("bk8_s", [128, 8], F32)
    brep = sb("brep_s", [128, D], F32)
    borep = brep
    bfrrep = brep
    bvfix = sb("bvfix_s", [1, H * 65], BF16)
    g1r = sb("g1r_s", [128, 32], BF16)
    g1i = sb("g1i_s", [128, 32], BF16)
    g1in = sb("g1in", [128, 32], BF16)
    onescol = sb("onescol", [128, 1], BF16)
    onesf = sb("onesf", [1, 128], F32)
    cones = sb("cones", [1, 128], BF16)
    onesrow = sb("onesrow", [1, M], BF16)
    ln116c = sb("ln116c", [128, 1], F32)
    normc = sb("normc", [128, 1], F32)
    lnepsc = sb("lnepsc", [128, 1], F32)

    pg = [ps("pg0", [128, 512], F32), ps("pg1", [128, 512], F32),
          ps("pg2", [128, 512], F32), ps("pg3", [128, 512], F32)]
    pk = [ps("pk0", [128, 512], F32), ps("pk1", [128, 512], F32)]
    pcf = [ps("pcA", [128, 512], F32), ps("pcB", [128, 512], F32)]
    pc = [pcf[0][0:65, :], pcf[1][0:65, :]]
    pall = pg + pk + pcf

    semnames = ["ld", "gld", "sld", "mm", "act", "dve", "gp", "cc", "tld", "cld", "wld", "xld", "yld"]
    S = {n: es.enter_context(nc.semaphore(n)) for n in semnames}

    xfm_v = d_xfm.rearrange("(c p) n -> p c n", p=128)

    def phases(c: Ctx):
        MS = {}

        c.dma("y", identf[:], d_ident)
        c.dma("y", bq8[:], d_bq)
        c.dma("y", bk8[:], d_bk)
        ld_consts = c.dma("y", borep[:], d_borep)
        c.dma("g", projtd[:], d_projtd)
        c.dma("g", half1[:], d_half1)
        c.dma("g", bvfix[:], d_bvfix)
        c.dma("g", g1r[:], d_g1r)
        gld_consts = c.dma("g", g1i[:], d_g1i)
        gld_wk = c.dma("g", wg[:], d_wkt.rearrange("(c p) o -> p c o", p=128), sem="wld")
        c.v("memset", onescol[:], 1.0)
        c.v("memset", onesf[:], 1.0)
        c.v("memset", cones[:], EPSC)
        c.v("memset", onesrow[:], 1.0)
        c.v("memset", ln116c[:], LN116)
        c.v("memset", normc[:], NORM)
        c.v("memset", lnepsc[:], LN_EPS)
        c.v("memset", maxcol[:], NEG_INF, inc=True)

        def slab_load(b, nch, wait_mm):
            c.w2("g", "mm", wait_mm)
            n0 = b * NB + nch * 512
            c.w2("g", "xld", c.L["xld"])
            return c.dma("g", xsb[nch % 2][:], xfm_v[:, :, n0:n0 + 512],
                         sem="xld")

        def proj_fm(b, dstv, bias8, gld_w, entry, first_slab_wait=0):
            evacs = []
            grp_mm = []
            g = 0
            for nch in range(4):
                ld_here = slab_load(b, nch,
                                    grp_mm[8 * (nch - 2) + 7] if nch >= 2
                                    else first_slab_wait)
                for oc in range(C8):
                    bank = pg[g % 4]
                    if g == 0:
                        for sem, val in entry:
                            c.w2("t", sem, val)
                    if oc == 0:
                        c.w2("t", "xld", ld_here)
                        c.w2("t", "wld", gld_w)
                    if g >= 4:
                        c.w2("t", "act", evacs[g - 4])
                    for ic in range(C8):
                        c.mm(bank[:, 0:512], wg[:, ic, oc * 128:oc * 128 + 128],
                             xsb[nch % 2][:, ic, 0:512],
                             start=(ic == 0), stop=(ic == 7), inc=(ic == 7))
                    grp_mm.append(c.L["mm"])
                    c.w2("s", "mm", grp_mm[-1])
                    if g == 0:
                        c.w2("s", "ld", ld_consts)
                    evacs.append(c.act(dstv[:, oc, nch * 512:nch * 512 + 512],
                                       bank[:, 0:512], AF.Identity,
                                       bias=bias8[:, oc:oc + 1], inc=True))
                    g += 1
            return grp_mm[-1], evacs[-1]

        def proj_tm_v(b, gld_w, entry, first_slab_wait=0):
            evacs = []
            grp_mm = []
            g = 0
            for nch in range(4):
                ld_here = slab_load(b, nch,
                                    grp_mm[8 * (nch - 2) + 7] if nch >= 2
                                    else first_slab_wait)
                for nt4 in range(4):
                    nt = nch * 4 + nt4
                    for oh in range(2):
                        bank = pg[g % 4]
                        if g == 0:
                            for sem, val in entry:
                                c.w2("t", sem, val)
                        if nt4 == 0 and oh == 0:
                            c.w2("t", "xld", ld_here)
                            c.w2("t", "wld", gld_w)
                        if g >= 4:
                            c.w2("t", "dve", evacs[g - 4])
                        for ic in range(C8):
                            c.mm(bank[:, 0:512],
                                 xsb[nch % 2][:, ic, nt4 * 128:nt4 * 128 + 128],
                                 wg[:, ic, oh * 512:oh * 512 + 512],
                                 start=(ic == 0), stop=(ic == 7), inc=(ic == 7))
                        grp_mm.append(c.L["mm"])
                        c.w2("v", "mm", grp_mm[-1])
                        dst = vv[b][:, nt, oh * 520:oh * 520 + 520]
                        dst = dst.rearrange("p (a e) -> p a e", a=8)[:, :, 0:64]
                        evacs.append(
                            c.v("tensor_copy", dst,
                                bank[:, 0:512].rearrange("p (a e) -> p a e", a=8),
                                inc=True))
                        g += 1
                    c.v("memset", vv[b][:, nt, 64:1040:65], 1.0, inc=True)
            return grp_mm[-1], c.L["dve"]

        def diag_phase(b, act_kevac):
            for nt in range(NT):
                buf = nt % 2
                if nt == 0:
                    c.w2("s", "act", act_kevac)
                if nt >= 2:
                    c.w2("s", "mm", MS[("dmm", b, nt - 2)])
                a_ksq = c.act(ksqsb[buf][:], kv[b][:, :, nt * 128:nt * 128 + 128],
                              AF.Square, inc=True)
                c.w2("t", "act", a_ksq)
                if nt >= 1:
                    c.w2("t", "dve", MS[("devac", b, nt - 1)])
                if b == 1 and nt == 0:
                    c.w2("t", "dve", MS[("devac", 0, NT - 1)])
                for oc in range(C8):
                    c.mm(pk[0][:, 0:16], ksqsb[buf][:, oc, :],
                         half1[:, oc * 16:oc * 16 + 16],
                         start=(oc == 0), stop=(oc == 7), inc=(oc == 7))
                MS[("dmm", b, nt)] = c.L["mm"]
                c.w2("v", "mm", MS[("dmm", b, nt)])
                col = b * 256 + nt * 16
                MS[("devac", b, nt)] = c.v("tensor_copy", diag[:, col:col + 16],
                                           pk[0][:, 0:16], inc=True)

        mm_k0, act_k0 = proj_fm(0, kv[0], bk8, gld_wk, entry=[])
        diag_phase(0, act_k0)
        mm_k1, act_k1 = proj_fm(1, kv[1], bk8, gld_wk,
                                entry=[("act", act_k0)], first_slab_wait=mm_k0)
        diag_phase(1, act_k1)

        c.w2("v", "dve", MS[("devac", 1, NT - 1)])
        dve_mc = c.v("tensor_reduce", maxcol[:], diag[:, 0:512],
                     op=ALU.max, axis=mybir.AxisListType.X, inc=True)
        c.w2("t", "dve", dve_mc)
        c.w2("t", "act", act_k1)
        c.w2("t", "ld", ld_consts)
        c.tr(pg[1][0:1, 0:128], maxcol[:], identf[:])
        mm_tp1 = c.mm(pg[3][0:1, 0:1], onesf[0:1, 0:1], onesf[0:1, 0:1],
                      start=True, stop=True, inc=True)
        c.w2("v", "mm", mm_tp1)
        c.v("tensor_copy", mx[:], pg[1][0:1, 0:128])
        dve_mloc = c.v("tensor_reduce", mloc[:], mx[:], op=ALU.max,
                       axis=mybir.AxisListType.X, inc=True)
        c.w2("g", "dve", dve_mloc)
        cld_ccin = c.dma("g", d_ccin, mloc[:], sem="cld")
        c.w2("g", "cld", cld_ccin)
        cc_done = c.coll([d_ccin], [d_ccout])
        c.w2("g", "cc", cc_done)
        cld_gat = c.dma("g", gat[:], d_ccout, sem="cld")
        c.w2("v", "cld", cld_gat)
        dve_st11 = c.v("tensor_reduce", st11[:], gat[:], op=ALU.max,
                       axis=mybir.AxisListType.X, inc=True)
        c.w2("t", "dve", dve_st11)
        mm_stb = c.mm(pg[2][:, 0:1], onesf[:], st11[:], start=True, stop=True,
                      inc=True)
        c.w2("v", "mm", mm_stb)
        dve_stc = c.v("tensor_copy", stabv[:], pg[2][:, 0:1], inc=True)
        c.w2("v", "dve", dve_stc)
        c.v("tensor_scalar", out=dadj[:, 0:512], in0=diag[:, 0:512],
            scalar1=stabv[:], scalar2=None, op0=ALU.subtract)
        MS["dadj"] = c.v("tensor_scalar", out=dadj[:, 0:512], in0=dadj[:, 0:512],
                         scalar1=LN116, scalar2=None, op0=ALU.add, inc=True)

        c.w2("g", "mm", mm_k1)
        gld_wq = c.dma("g", wg[:], d_wqt.rearrange("(c p) o -> p c o", p=128), sem="wld")
        mm_q0, act_q0 = proj_fm(0, qv[0], bq8, gld_wq,
                                entry=[("act", act_k1), ("dve", MS["dadj"])],
                                first_slab_wait=mm_k1)
        c.w2("g", "mm", mm_q0)
        gld_wv = c.dma("g", wg[:], d_wvt.rearrange("(c p) o -> p c o", p=128), sem="wld")
        mm_v0, dve_v0 = proj_tm_v(0, gld_wv, entry=[("act", act_q0)],
                                  first_slab_wait=mm_q0)
        MS[("vready", 0)] = dve_v0

        def p5(b):
            c.w2("t", "dve", MS[("vready", b)])
            ranges = [(0, 512), (512, 512), (1024, 16)]
            for gi, (c0, ln) in enumerate(ranges):
                bank = pg[gi]
                for nt in range(NT):
                    c.mm(bank[0:1, 0:ln], onescol[:], vv[b][:, nt, c0:c0 + ln],
                         start=(nt == 0), stop=(nt == NT - 1), inc=(nt == NT - 1))
                mmv = c.L["mm"]
                c.w2("v", "mm", mmv)
                c.v("tensor_copy", vsum[0:1, c0:c0 + ln], bank[0:1, 0:ln])
            MS[("vsc", b)] = c.v("tensor_scalar", out=vsc[0:1, :],
                                 in0=vsum[0:1, :], scalar1=EPSC, scalar2=None,
                                 op0=ALU.mult, inc=True)

            kslot = [pk[0][:, 0:256], pk[1][:, 0:256], pg[3][:, 0:256]]
            for h in range(H):
                par = h % 2
                chunk = h // 2
                prow = slice(64 * par, 64 * par + 64)
                pcx = pc[h % 2]
                for nt in range(NT):
                    ks = nt % 3
                    kbuf = nt % 2
                    if nt >= 3:
                        c.w2("t", "act", MS[("kfexp", b, h, nt - 3)])
                    elif h >= 1:
                        c.w2("t", "act", MS[("kfexp", b, h - 1, NT - 3 + nt)])
                    if nt == 0 and h >= 2:
                        c.w2("t", "dve", MS[("ctxTev", b, h - 2)])
                    c.mm(kslot[ks],
                         kv[b][prow, chunk, nt * 128:nt * 128 + 128],
                         projtd[prow, 0:256],
                         start=True, stop=True, inc=True)
                    MS[("kdmm", b, h, nt)] = c.L["mm"]
                    c.w2("s", "mm", MS[("kdmm", b, h, nt)])
                    if h == 0 and nt == 0:
                        c.w2("s", "dve", MS["dadj"])
                    col = b * 256 + nt * 16 + h
                    MS[("kfexp", b, h, nt)] = c.act(
                        kfb[kbuf][:], kslot[ks], AF.Exp,
                        bias=dadj[:, col:col + 1], scale=normc[:], inc=True)
                    c.w2("t", "act", MS[("kfexp", b, h, nt)])
                    if h == 0 and nt == 0:
                        c.w2("t", "dve", MS[("vsc", b)])
                    c.mm(pcx[0:65, 0:256], vv[b][:, nt, h * 65:h * 65 + 65],
                         kfb[kbuf][:], start=(nt == 0), stop=False,
                         inc=(nt == NT - 1))
                MS[("ctxmm", b, h)] = c.L["mm"]
                mm_v1f = c.mm(pcx[0:65, 0:256], vsc[0:1, h * 65:h * 65 + 65],
                              onesrow[:], start=False, stop=False, inc=True)
                c.w2("v", "mm", mm_v1f)
                dve_kcs = c.v("tensor_copy", kcsrow[:], pcx[64:65, 0:256],
                              inc=True)
                c.w2("t", "dve", dve_kcs)
                mm_fix = c.mm(pcx[0:65, 0:256], bvfix[0:1, h * 65:h * 65 + 65],
                              kcsrow[:], start=False, stop=True, inc=True)
                c.w2("v", "mm", mm_fix)
                MS[("ctxTev", b, h)] = c.v("tensor_copy", ctxT[:],
                                           pcx[0:65, 0:256], inc=True)
                c.w2("t", "dve", MS[("ctxTev", b, h)])
                c.tr(pg[0][0:128, 0:65], ctxT[:, 0:128], identf[0:65, 0:65])
                c.tr(pg[1][0:128, 0:65], ctxT[:, 128:256], identf[0:65, 0:65])
                mm_t1 = c.mm(pg[2][0:1, 0:1], onesf[0:1, 0:1], onesf[0:1, 0:1],
                             start=True, stop=True, inc=True)
                c.w2("v", "mm", mm_t1)
                c.v("tensor_copy", ctxbf[:, h, 0:65], pg[0][0:128, 0:65])
                dve_ctx = c.v("tensor_copy", ctxbf[:, h, 65:130],
                              pg[1][0:128, 0:65], inc=True)
                c.w2("t", "dve", dve_ctx)
                c.mm(pg[2][0:1, 0:65], onescol[:], ctxbf[:, h, 0:65],
                     start=True, stop=False)
                mm_s = c.mm(pg[2][0:1, 0:65], onescol[:], ctxbf[:, h, 65:130],
                            start=False, stop=True, inc=True)
                c.w2("v", "mm", mm_s)
                MS[("srow", b, h)] = c.v("tensor_copy",
                                         srow[0:1, h * 65:h * 65 + 65],
                                         pg[2][0:1, 0:65], inc=True)

        def p6(b):
            def qdash_block(h):
                par = h % 2
                chunk = h // 2
                prow = slice(64 * par, 64 * par + 64)
                for mch in range(2):
                    for nch in range(4):
                        gi = mch * 4 + nch
                        bank = pk[gi % 2]
                        if gi >= 2:
                            c.w2("t", "act", MS[("qfexp", b, h, gi - 2)])
                        elif h >= 1:
                            c.w2("t", "act", MS[("qfexp", b, h - 1, 6 + gi)])
                        else:
                            c.w2("t", "act", MS[("kfexp", b, H - 1, NT - 1)])
                        c.mm(bank[:, 0:512],
                             projtd[prow, mch * 128:mch * 128 + 128],
                             qv[b][prow, chunk, nch * 512:nch * 512 + 512],
                             start=True, stop=True, inc=True)
                        MS[("qdmm", b, h, gi)] = c.L["mm"]
                        c.w2("s", "mm", MS[("qdmm", b, h, gi)])
                        if gi == 0 and h >= 2:
                            c.w2("s", "mm", MS[("outmm", b, h - 2)])
                        MS[("qfexp", b, h, gi)] = c.act(
                            qfb[h % 2][:, mch, nch * 512:nch * 512 + 512],
                            bank[:, 0:512], AF.Exp, bias=ln116c[:],
                            scale=normc[:], inc=True)

            def out_block(h):
                c.w2("t", "act", MS[("qfexp", b, h, 7)])
                if h == 0:
                    c.w2("t", "dve", MS[("srow", b, H - 1)])
                else:
                    c.w2("t", "dve", MS[("rawev", b, h - 1)])
                for nt in range(NT):
                    bank = pg[nt // 4]
                    off = 128 * (nt % 4)
                    c.mm(bank[:, off:off + 65],
                         qfb[h % 2][:, 0, nt * 128:nt * 128 + 128],
                         ctxbf[:, h, 0:65], start=True, stop=False)
                    c.mm(bank[:, off:off + 65],
                         qfb[h % 2][:, 1, nt * 128:nt * 128 + 128],
                         ctxbf[:, h, 65:130], start=False, stop=False)
                    c.mm(bank[:, off:off + 65], cones[:],
                         srow[0:1, h * 65:h * 65 + 65], start=False, stop=True,
                         inc=(nt == NT - 1))
                MS[("outmm", b, h)] = c.L["mm"]
                c.w2("v", "mm", MS[("outmm", b, h)])
                for qb in range(4):
                    c.v("reciprocal", dinv[:, 4 * qb:4 * qb + 4],
                        pg[qb][:, 64:512:128], inc=(qb == 3))
                c.w2("v", "dve", c.L["dve"])
                for nt in range(NT):
                    bank = pg[nt // 4]
                    off = 128 * (nt % 4)
                    c.v("tensor_scalar",
                        out=rawv[b][:, nt, 64 * h:64 * h + 64],
                        in0=bank[:, off:off + 64], scalar1=dinv[:, nt:nt + 1],
                        scalar2=None, op0=ALU.mult, inc=(nt == NT - 1))
                MS[("rawev", b, h)] = c.L["dve"]

            for h in range(H):
                qdash_block(h)
                out_block(h)
            c.w2("s", "dve", MS[("rawev", b, H - 1)])
            for ch in range(C8):
                sld = c.dma("s",
                            d_attnh[ch][b * NB:(b + 1) * NB, :]
                            .rearrange("(t p) f -> p t f", p=128),
                            rawv[b][:, :, ch * 128:ch * 128 + 128])
            MS[("rawh", b)] = sld

        p5(0)
        p6(0)

        c.w2("g", "mm", mm_v0)
        gld_wq1 = c.dma("g", wg[:], d_wqt.rearrange("(c p) o -> p c o", p=128), sem="wld")
        mm_q1, act_q1 = proj_fm(1, qv[1], bq8, gld_wq1,
                                entry=[("dve", MS[("rawev", 0, H - 1)])],
                                first_slab_wait=mm_v0)
        c.w2("g", "mm", mm_q1)
        gld_wv1 = c.dma("g", wg[:], d_wvt.rearrange("(c p) o -> p c o", p=128), sem="wld")
        mm_v1, dve_v1 = proj_tm_v(1, gld_wv1, entry=[("act", act_q1)],
                                  first_slab_wait=mm_q1)
        MS[("vready", 1)] = dve_v1

        p5(1)
        p6(1)

        c.w2("g", "mm", MS[("ctxmm", 1, H - 1)])
        gld_tfr = c.dma("g", tfr_s[:], d_tfr, sem="wld")
        gld_tfi = c.dma("g", tfi_s[:], d_tfi, sem="wld")
        c.w2("v", "gld", gld_consts)
        dve_g1n = c.v("tensor_scalar", out=g1in[:], in0=g1i[:], scalar1=-1.0,
                      scalar2=None, op0=ALU.mult, inc=True)
        wt_evacs = []
        gi = 0
        for rhalf in range(2):
            for j in range(4):
                for ch in range(2):
                    bank = pk[gi % 2]
                    lhs = [g1r, g1i][ch] if rhalf == 0 else [g1in, g1r][ch]
                    rhs_t = [tfr_s, tfi_s][ch] if rhalf == 0 else [tfi_s, tfr_s][ch]
                    if gi == 0:
                        c.w2("t", "wld", gld_tfi)
                        c.w2("t", "dve", dve_g1n)
                        c.w2("t", "act", MS[("qfexp", 1, H - 1, 7)])
                    if gi >= 2:
                        c.w2("t", "dve", wt_evacs[gi - 2])
                    for g4 in range(4):
                        gg = 4 * j + g4
                        c.mm(bank[32 * g4:32 * g4 + 32, 0:512], lhs[:],
                             rhs_t[:, gg * 512:gg * 512 + 512],
                             start=True, stop=True, tp=(0, 32 * g4),
                             inc=(g4 == 3))
                    mmw = c.L["mm"]
                    c.w2("v", "mm", mmw)
                    wt_evacs.append(
                        c.v("tensor_copy",
                            wtv[ch][:, rhalf * 4 + j, :],
                            bank[:, 0:512], inc=True))
                    gi += 1
        MS["wtdone"] = wt_evacs[-1]
        MS["wtmm"] = c.L["mm"]

        def ln_chain(phase, tile, grpA, grpB, bankA, bankB, store_list):
            s0 = 16 * (tile % 2)
            c.w2("v", "mm", max(grpA, grpB))
            c.w2("v", "gp", MS[("tb", phase, tile)])
            if phase == 8 and tile < 2:
                prev_app = MS[("app", 7, 30 + tile)]
            else:
                prev_app = MS.get(("app", phase, tile - 2), 0)
            c.w2("v", "act", prev_app)
            if phase == 7 and tile == 0:
                c.w2("v", "sld", MS[("rawh", 1)])
            pre = x1pre[tile % 2]
            c.v("tensor_add", pre[:, 0:512], tbuf[tile % 2][:, 0:512],
                bankA[:, 0:512])
            dve_pre = c.v("tensor_add", pre[:, 512:1024],
                          tbuf[tile % 2][:, 512:1024], bankB[:, 0:512],
                          inc=True)
            MS[("prefree", phase, tile)] = dve_pre
            gp_rs = c.v("tensor_reduce", lnb[:, s0 + 0:s0 + 1], pre[:],
                        op=ALU.add, axis=mybir.AxisListType.X, inc=True)
            c.w2("s", "dve", dve_pre)
            c.act(sqscr[:], pre[:], AF.Square,
                  accum_out=lnb[:, s0 + 1:s0 + 2])
            if phase == 7 and tile == 0:
                c.w2("s", "sld", MS[("rawh", 1)])
            a_sq = c.act(lnb[:, s0 + 9:s0 + 10], lnb[:, s0 + 0:s0 + 1],
                         AF.Identity, inc=True)
            c.w2("v", "act", a_sq)
            c.v("tensor_scalar", out=lnb[:, s0 + 2:s0 + 3],
                in0=lnb[:, s0 + 0:s0 + 1], scalar1=1.0 / 1024.0, scalar2=None,
                op0=ALU.mult)
            c.v("tensor_scalar", out=lnb[:, s0 + 3:s0 + 4],
                in0=lnb[:, s0 + 1:s0 + 2], scalar1=1.0 / 1024.0, scalar2=None,
                op0=ALU.mult)
            c.v("tensor_mul", lnb[:, s0 + 4:s0 + 5], lnb[:, s0 + 2:s0 + 3],
                lnb[:, s0 + 2:s0 + 3])
            dve_var = c.v("tensor_sub", lnb[:, s0 + 5:s0 + 6],
                          lnb[:, s0 + 3:s0 + 4], lnb[:, s0 + 4:s0 + 5],
                          inc=True)
            c.w2("s", "dve", dve_var)
            a_std = c.act(lnb[:, s0 + 6:s0 + 7], lnb[:, s0 + 5:s0 + 6],
                          AF.Sqrt, bias=lnepsc[:], inc=True)
            c.w2("v", "act", a_std)
            dve_rec = c.v("reciprocal", lnb[:, s0 + 7:s0 + 8],
                          lnb[:, s0 + 6:s0 + 7], inc=True)
            c.w2("v", "dve", dve_rec)
            dve_inv = c.v("tensor_scalar", out=lnb[:, s0 + 8:s0 + 9],
                          in0=lnb[:, s0 + 2:s0 + 3],
                          scalar1=lnb[:, s0 + 7:s0 + 8], scalar2=-1.0,
                          op0=ALU.mult, op1=ALU.mult, inc=True)
            c.w2("s", "dve", dve_inv)
            if len(store_list) >= 2 or phase == 8:
                c.w2("s", "sld", c.L["sld"])
            a_app = c.act(x1t[tile % 2], pre[:], AF.Identity,
                          bias=lnb[:, s0 + 8:s0 + 9],
                          scale=lnb[:, s0 + 7:s0 + 8], inc=True)
            MS[("app", phase, tile)] = a_app
            return a_app

        x1_sld = []
        c.w2("y", "sld", MS[("rawh", 1)])
        c.w2("y", "mm", MS["wtmm"])
        for half in range(2):
            if half == 1:
                c.w2("y", "mm", MS[("womm", 0, NT - 1, 1)])
            for ch in range(C8):
                ld_art = c.dma("y", art[:, ch, :],
                               d_attnh[ch][half * NB:(half + 1) * NB, :],
                               transpose=True, sem="tld")
            MS[("artld", half)] = ld_art
            if half == 0:
                gld_wo = c.dma("g", wg[:],
                               d_wot.rearrange("(c p) o -> p c o", p=128),
                               sem="wld")
                MS["gldwo"] = gld_wo
            for nt in range(NT):
                tile = half * NT + nt
                c.w2("y", "gp", MS.get(("tb", 7, tile - 2), 0))
                c.w2("y", "sld", MS.get(("rawh", 1), 0))
                t0 = tile * 128
                c.w2("y", "yld", c.L["yld"])
                ld_x = c.dma("y", xtmb[tile % 2], d_xtm[t0:t0 + 128, :],
                             sem="yld")
                MS[("xtmld", tile)] = ld_x
                for oh in range(2):
                    bank = pg[(2 * nt + oh) % 4]
                    if nt == 0 and oh == 0:
                        c.w2("t", "wld", MS["gldwo"])
                        c.w2("t", "tld", MS[("artld", half)])
                        if half == 0:
                            c.w2("t", "dve", max(MS["wtdone"],
                                                 MS[("rawev", 1, H - 1)]))
                    if tile >= 2 or (half == 1 and nt < 2):
                        c.w2("t", "dve", MS.get(("prefree", 7, tile - 2), 0))
                    for ic in range(C8):
                        c.mm(bank[:, 0:512],
                             art[:, ic, nt * 128:nt * 128 + 128],
                             wg[:, ic, oh * 512:oh * 512 + 512],
                             start=(ic == 0), stop=(ic == 7), inc=(ic == 7))
                    MS[("womm", half, nt, oh)] = c.L["mm"]
                c.w2("g", "yld", MS[("xtmld", tile)])
                c.w2("g", "ld", ld_consts)
                c.w2("g", "dve", MS.get(("prefree", 7, tile - 2), 0))
                MS[("tb", 7, tile)] = c.g("tensor_add", tbuf[tile % 2],
                                          xtmb[tile % 2], borep[:], inc=True)
                a_app = ln_chain(7, tile, MS[("womm", half, nt, 0)],
                                 MS[("womm", half, nt, 1)],
                                 pg[(2 * nt) % 4], pg[(2 * nt + 1) % 4],
                                 x1_sld)
                c.w2("s", "act", a_app)
                x1_sld.append(c.dma("s", d_x1h[tile * 128:tile * 128 + 128, :],
                                    x1t[tile % 2]))
                MS[("x1sld", tile)] = x1_sld[-1]
                c.w2("v", "act", a_app)
                if tile == 0:
                    c.w2("v", "sld", MS[("rawh", 1)])
                if tile % 4 == 0 and tile >= 4:
                    c.w2("v", "sld", c.L["sld"])
                dve_cast = c.v("tensor_copy", x1bfb[:, tile % 4, :],
                               x1t[tile % 2], inc=True)
                if tile % 4 == 3:
                    g0 = (tile // 4) * 4
                    c.w2("s", "dve", dve_cast)
                    for ch in range(C8):
                        sldx = c.dma(
                            "s",
                            d_x1bh[ch][g0 * 128:g0 * 128 + 512, :]
                            .rearrange("(t p) f -> p t f", p=128),
                            x1bfb[:, :, ch * 128:ch * 128 + 128])
                    MS[("x1bhw", tile // 4)] = sldx
        MS["x1h_all"] = x1_sld[-1]

        y_sld = []
        c.w2("y", "gp", MS[("tb", 7, 2 * NT - 1)])
        ld_bfr = c.dma("y", bfrrep[:], d_bfrrep)
        for half in range(2):
            c.w2("y", "sld", MS[("x1bhw", 7)])
            c.w2("y", "mm", MS[("womm", 1, NT - 1, 1)])
            if half == 1:
                c.w2("y", "mm", MS[("ffmm", 0, NT - 1, 1)])
            for ch in range(C8):
                ld_xf = c.dma("y", art[:, ch, :],
                              d_x1bh[ch][half * NB:(half + 1) * NB, :],
                              transpose=True, sem="tld")
            MS[("xftld", half)] = ld_xf
            for nt in range(NT):
                tile = half * NT + nt
                c.w2("y", "gp", MS.get(("tb", 8, tile - 2),
                                       MS[("tb", 7, 30 + tile)] if tile < 2 else 0))
                c.w2("y", "sld", MS["x1h_all"])
                t0 = tile * 128
                c.w2("y", "yld", c.L["yld"])
                ld_x1 = c.dma("y", xtmb[tile % 2], d_x1h[t0:t0 + 128, :],
                              sem="yld")
                MS[("x1ld", tile)] = ld_x1
                for oh in range(2):
                    bank = pg[(2 * nt + oh) % 4]
                    if nt == 0 and oh == 0:
                        c.w2("t", "tld", MS[("xftld", half)])
                        c.w2("t", "dve", MS["wtdone"])
                        if half == 0:
                            c.w2("t", "dve", MS[("prefree", 7, 2 * NT - 1)])
                    if tile >= 2 or (half == 1 and nt < 2):
                        c.w2("t", "dve", MS.get(("prefree", 8, tile - 2), 0))
                    for ic in range(C8):
                        c.mm(bank[:, 0:512],
                             art[:, ic, nt * 128:nt * 128 + 128],
                             wtv[oh][:, ic, :],
                             start=(ic == 0), stop=(ic == 7), inc=(ic == 7))
                    MS[("ffmm", half, nt, oh)] = c.L["mm"]
                c.w2("g", "yld", MS[("x1ld", tile)])
                c.w2("g", "ld", ld_bfr)
                c.w2("g", "dve", MS.get(("prefree", 8, tile - 2),
                                        MS[("prefree", 7, 30 + tile)]
                                        if tile < 2 else 0))
                MS[("tb", 8, tile)] = c.g("tensor_add", tbuf[tile % 2],
                                          xtmb[tile % 2], bfrrep[:], inc=True)
                a_app = ln_chain(8, tile, MS[("ffmm", half, nt, 0)],
                                 MS[("ffmm", half, nt, 1)],
                                 pg[(2 * nt) % 4], pg[(2 * nt + 1) % 4],
                                 y_sld)
                c.w2("s", "act", a_app)
                y_sld.append(c.dma("s", d_y[tile * 128:tile * 128 + 128, :],
                                   x1t[tile % 2]))
        c.w2("s", "sld", y_sld[-1])

    with nc.Block() as blk:
        blk.tensor(lambda e: phases(Ctx("t", e, S)))
        blk.scalar(lambda e: phases(Ctx("s", e, S)))
        blk.vector(lambda e: phases(Ctx("v", e, S)))
        blk.gpsimd(lambda e: phases(Ctx("g", e, S)))
        blk.sync(lambda e: phases(Ctx("y", e, S)))
    es.close()
    return nc


_NC = None


def _get_nc():
    global _NC
    if _NC is None:
        _NC = build_nc()
    return _NC


def _sigma_full():
    p = np.arange(512)
    s = (p % 32) * 16 + p // 32
    return np.concatenate([s, 512 + s])


def kernel(L0_attn_hidden_states, L0_attn_Wq, L0_attn_bq, L0_attn_Wk, L0_attn_bk,
           L0_attn_Wv, L0_attn_bv, L0_attn_Wo, L0_attn_bo, L0_attn_proj,
           L0_ffn_G1_R, L0_ffn_G2_R, L0_ffn_G1_I, L0_ffn_G2_I,
           L0_ffn_bias_real, L0_ffn_bias_imag):
    x = np.asarray(L0_attn_hidden_states, dtype=np.float32)
    Wq = np.asarray(L0_attn_Wq, np.float32)
    Wk = np.asarray(L0_attn_Wk, np.float32)
    Wv = np.asarray(L0_attn_Wv, np.float32)
    Wo = np.asarray(L0_attn_Wo, np.float32)
    bq = np.asarray(L0_attn_bq, np.float32)
    bk = np.asarray(L0_attn_bk, np.float32)
    bv = np.asarray(L0_attn_bv, np.float32)
    bo = np.asarray(L0_attn_bo, np.float32)
    proj = np.asarray(L0_attn_proj, np.float32)
    G1R = np.asarray(L0_ffn_G1_R, np.float32)
    G2R = np.asarray(L0_ffn_G2_R, np.float32)
    G1I = np.asarray(L0_ffn_G1_I, np.float32)
    G2I = np.asarray(L0_ffn_G2_I, np.float32)
    br = np.asarray(L0_ffn_bias_real, np.float32)
    bi = np.asarray(L0_ffn_bias_imag, np.float32)

    sg = _sigma_full()
    C = np.ascontiguousarray

    wqt = C(Wq.T[sg])
    wkt = C(Wk.T[sg])
    wvt = C(Wv.T[sg])
    wot = C(Wo.T[:, sg])
    projtd = C(np.concatenate([proj.T, proj.T], axis=0))
    half1 = np.zeros((128, 128), np.float32)
    for oc in range(8):
        half1[0:64, oc * 16 + 2 * oc] = -0.5
        half1[64:128, oc * 16 + 2 * oc + 1] = -0.5
    ident = np.eye(128, dtype=np.float32)
    bq8 = C(bq.reshape(8, 128).T)
    bk8 = C(bk.reshape(8, 128).T)
    borep = C(np.broadcast_to(bo[sg], (128, 1024)))
    bfr = np.concatenate([br, bi])[sg]
    bfrrep = C(np.broadcast_to(bfr, (128, 1024)))
    bvfix = np.zeros((1, 16 * 65), np.float32)
    for h in range(16):
        bvfix[0, 65 * h:65 * h + 64] = bv[64 * h:64 * h + 64]
    g1r = C(G1R[0].transpose(1, 0, 2).reshape(128, 32))
    g1i = C(G1I[0].transpose(1, 0, 2).reshape(128, 32))

    def tfull(G2):
        T = np.zeros((128, 16, 512), np.float32)
        for cc in range(4):
            for bb in range(32):
                for e in range(16):
                    T[cc * 32 + bb, :, e * 32 + bb] = G2[cc, e, 0, :]
        return C(T.reshape(128, 16 * 512))

    tfr = tfull(G2R)
    tfi = tfull(G2I)

    nc = _get_nc()
    in_maps = []
    for core in range(8):
        xp = x[2 * core:2 * core + 2]
        xtm = C(xp.reshape(4096, 1024)[:, sg])
        xfm = C(xp.reshape(4096, 1024).T[sg])
        in_maps.append(dict(
            xfm=xfm, xtm=xtm, wqt=wqt, wkt=wkt, wvt=wvt, wot=wot,
            projtd=projtd, half1=half1, ident=ident, bq8=bq8, bk8=bk8,
            borep=borep, bfrrep=bfrrep, bvfix=bvfix, g1r=g1r, g1i=g1i,
            tfr=tfr, tfi=tfi,
        ))
    res = run_bass_kernel_spmd(nc, in_maps, core_ids=list(range(8)))
    inv = np.argsort(sg)
    outs = []
    for core in range(8):
        yc = np.asarray(res.results[core]["y"])
        outs.append(yc[:, inv].reshape(2, 2048, 1024))
    return np.concatenate(outs, axis=0)
